# version 01 (final)
# baseline (speedup 1.0000x reference)
import os
import numpy as np
import ml_dtypes
from contextlib import ExitStack

import concourse.bass as bass
import concourse.mybir as mybir
from concourse.bass_utils import run_bass_kernel_spmd

F32 = mybir.dt.float32
BF16 = mybir.dt.bfloat16
I32 = mybir.dt.int32
AF = mybir.ActivationFunctionType
ALU = mybir.AluOpType

D = 2048
S = 2048
TOK = 1024
DH = 64
NP = 16
NG = 4
KEXT = 70
MLP = 16384
WSLICE = 16 * 16 * 16384
CSLICE = 3 * WSLICE
DVE_TILES = list(range(0, 16))
POOL_TILES = list(range(16, 48))
def _mk_tr_order():
    order = []
    pi, di = 0, 0
    while pi < len(POOL_TILES) or di < len(DVE_TILES):
        for _ in range(2):
            if pi < len(POOL_TILES):
                order.append(POOL_TILES[pi])
                pi += 1
        if di < len(DVE_TILES):
            order.append(DVE_TILES[di])
            di += 1
    return order


TR_ORDER = _mk_tr_order()

PERM_D = np.concatenate([np.arange(0, D, 2), np.arange(1, D, 2)])
PERM_N = np.concatenate([np.arange(0, MLP, 2), np.arange(1, MLP, 2)])

STAGE_FULL = 9
MAPB = 20480

LAYOUT = {
    "ones_r": (0, [1, 128], F32),
    "ones_c": (512, [128, 1], F32),
    "psw_s": (576, [128, 128], BF16),
    "idn_s": (832, [128, 128], BF16),
    "mtri_s": (1088, [128, 128], F32),
    "bo_s": (1600, [128, 16], F32),
    "off_s": (1664, [1, 1], I32),
    "c_m30": (1728, [128, 1], F32),
    "pkb0": (2048, [128, 1024], I32), "pkb1": (6144, [128, 1024], I32),
    "pkb2": (10240, [128, 1024], I32), "pkb3": (14336, [128, 1024], I32),
    "sclb0": (18432, [128, 64], F32), "sclb1": (18688, [128, 64], F32),
    "sclb2": (18944, [128, 64], F32), "sclb3": (19200, [128, 64], F32),
    "dcb0": (19456, [128, 2048], BF16), "dcb1": (23552, [128, 2048], BF16),
    "dcb2": (27648, [128, 2048], BF16), "dcb3": (31744, [128, 2048], BF16),
    "dqi0": (35840, [128, 1024], I32),
    "dqf0": (39936, [128, 1024], F32),
    "dqg0": (44032, [128, 1024], F32),
    "trstb0": (48128, [128, 2048], BF16), "trstb1": (52224, [128, 2048], BF16),
    "trstb2": (56320, [128, 2048], BF16), "trstb3": (60416, [128, 2048], BF16),
    "x_st": (68608, [128, 16 * 256], F32),
    "sq_st": (84992, [128, 256], F32),
    "t_st": (86016, [128, 256], F32),
    "xnst": (87040, [128, 16 * 256], BF16),
    "m_row": (95232, [1, 256], F32),
    "v_row": (96256, [1, 256], F32),
    "r_row": (97280, [1, 256], F32),
    "kt0": (68608, [KEXT, S], BF16), "kt1": (72704, [KEXT, S], BF16),
    "kt2": (76800, [KEXT, S], BF16), "kt3": (80896, [KEXT, S], BF16),
    "v_tm": (84992, [128, NG * 16 * 66], BF16),
    "qr0": (98304, [128, S], BF16), "qr1": (102400, [128, S], BF16),
    "qr2": (106496, [128, S], BF16), "qr3": (110592, [128, S], BF16),
    "qr4": (114688, [128, S], BF16), "qr5": (118784, [128, S], BF16),
    "qr6": (122880, [128, S], BF16), "qr7": (126976, [128, S], BF16),
    "wqm": (131072, [128, 2 * 2048], BF16),
    "xn_st": (139264, [128, 16 * 512], BF16),
    "k_raw": (155648, [128, 2 * S], BF16),
    "v_raw": (163840, [128, 2 * S], BF16),
    "rc_s": (172032, [128, S], F32),
    "rs_s": (180224, [128, S], F32),
    "ro1": (188416, [128, 512], F32),
    "ro2": (190464, [128, 512], F32),
    "qt0": (131072, [KEXT, S], BF16), "qt1": (135168, [KEXT, S], BF16),
    "ss0": (139264, [128, 1024], F32), "ss1": (143360, [128, 1024], F32),
    "pb0": (147456, [128, 1024], BF16), "pb1": (149504, [128, 1024], BF16),
    "rr_s": (151552, [1, S], F32),
    "rb_s": (159744, [64, S], F32),
    "ctxst": (167936, [128, S], BF16),
    "wom": (2048, [128, 2 * 4096], BF16),
    "cg_t": (18432, [128, 16 * 1024], BF16),
    "s_stage": (51200, [128, 2 * 1024], BF16),
    "xno": (55296, [128, 16 * 1024], BF16),
    "itT": (88064, [128, 16 * 1024], BF16),
    "wgb": (120832, [128, 2 * 4096], BF16),
    "wub": (137216, [128, 2 * 4096], BF16),
    "wdb": (153600, [128, 8 * 2048], BF16),
    "gel": (186368, [128, 2 * 256], F32),
    "itst": (188416, [128, 2 * 256], BF16),
    "yst": (189440, [128, 2 * 1024], F32),
    "ye_s": (2048, [128, 2 * 1024], F32),
    "xe_s": (10240, [128, 2 * 1024], F32),
    "se_s": (18432, [128, 2 * 1024], BF16),
    "ge_s": (22528, [128, 2 * 1024], F32),
    "oe_s": (30720, [128, 2 * 1024], F32),
}


def pieces(kc):
    c0 = kc * 128
    if c0 < 1024:
        return [(c0, 1024), (1024, 2048)]
    return [(c0, 2048)]


NPIECES = sum(len(pieces(kc)) for kc in range(16))


class Ck:
    def __init__(self, prev=None):
        self.d = {}
        self.prev = prev

    def put(self, key, v):
        self.d[key] = v

    def get(self, key):
        if self.prev is None:
            return 0
        return self.prev.d.get(key, 0)


def build_nc(stage=STAGE_FULL, ck_prev=None):
    nc = bass.Bass("TRN2", target_bir_lowering=False,
                   detect_race_conditions=bool(os.environ.get("K_RACES")))
    ck = Ck(ck_prev)
    led = {}

    def inc(s, n=1):
        led[s] = led.get(s, 0) + n
        return led[s]

    def val(s):
        return led.get(s, 0)

    xT = nc.dram_tensor("xT", [D, S], F32, kind="ExternalInput").ap()
    wq = nc.dram_tensor("wq", [D, NP * DH], F32, kind="ExternalInput").ap()
    wk = nc.dram_tensor("wk", [D, NG * DH], F32, kind="ExternalInput").ap()
    wv = nc.dram_tensor("wv", [D, NG * DH], F32, kind="ExternalInput").ap()
    wo = nc.dram_tensor("wo", [D, D], F32, kind="ExternalInput").ap()
    bo = nc.dram_tensor("bo", [128, 16], F32, kind="ExternalInput").ap()
    qa = nc.dram_tensor("qa", [NP, 6, S], BF16, kind="ExternalInput").ap()
    ka = nc.dram_tensor("ka", [6, S], BF16, kind="ExternalInput").ap()
    rcos = nc.dram_tensor("rcos", [128, S], F32, kind="ExternalInput").ap()
    rsin = nc.dram_tensor("rsin", [128, S], F32, kind="ExternalInput").ap()
    mtri = nc.dram_tensor("mtri", [128, 128], F32, kind="ExternalInput").ap()
    psw = nc.dram_tensor("psw", [128, 128], BF16, kind="ExternalInput").ap()
    idn = nc.dram_tensor("idn", [128, 128], BF16, kind="ExternalInput").ap()
    offi = nc.dram_tensor("offi", [1, 1], I32, kind="ExternalInput").ap()
    gp = nc.dram_tensor("gp", [2048, 1024], I32, kind="ExternalInput").ap()
    gs = nc.dram_tensor("gs", [2048, 64], F32, kind="ExternalInput").ap()
    upk = nc.dram_tensor("upk", [2048, 1024], I32, kind="ExternalInput").ap()
    us = nc.dram_tensor("us", [2048, 64], F32, kind="ExternalInput").ap()
    dp = nc.dram_tensor("dp", [2048, 1024], I32, kind="ExternalInput").ap()
    ds = nc.dram_tensor("ds", [2048, 64], F32, kind="ExternalInput").ap()
    outT = nc.dram_tensor("outT", [D, TOK], F32, kind="ExternalOutput").ap()

    def internal(name, shape, dtype, shared=False, out=False):
        if os.environ.get("K_DBG") == name:
            out = True
        return nc.dram_tensor(
            name, shape, dtype,
            kind="ExternalOutput" if out else "Internal",
            addr_space="Shared" if shared else "Local").ap()

    xn_d = internal("xn_d", [D, S], BF16, out=(stage == 1))
    q_d = internal("q_d", [NP * DH, S], BF16, out=(stage == 2))
    k_d = internal("k_d", [NG * DH, S], BF16, out=(stage == 2))
    v_d = internal("v_d", [NG * DH, S], BF16, out=(stage == 2))
    ctx_d = internal("ctx_d", [NP * DH, S], BF16, out=(stage == 3))
    ctx_g = internal("ctx_g", [2 * NP * DH, S], BF16)
    s_d = internal("s_d", [D, TOK], BF16, out=(stage == 4))
    deq = internal("deq", [CSLICE // 2048, 2048], BF16, out=(stage == 5))
    deq_g2 = internal("deq_g2", [2 * CSLICE // 2048, 2048], BF16)
    deq_g4 = internal("deq_g4", [4 * CSLICE // 2048, 2048], BF16)
    deq_g = internal("deq_g", [8 * CSLICE // 2048, 2048], BF16)
    yd = internal("yd", [D, TOK], F32, out=(stage == 6))
    xno_d = internal("xno_d", [D, TOK], BF16)
    cgo_d = internal("cgo_d", [D, TOK], BF16)
    xo_d = internal("xo_d", [D, TOK], F32)

    T = dict(xT=xT, wq=wq, wk=wk, wv=wv, wo=wo, bo=bo, qa=qa, ka=ka,
             rcos=rcos, rsin=rsin, mtri=mtri, psw=psw, idn=idn, offi=offi,
             xn_d=xn_d, q_d=q_d, k_d=k_d, v_d=v_d, ctx_d=ctx_d, ctx_g=ctx_g,
             s_d=s_d, deq=deq, deq_g2=deq_g2, deq_g4=deq_g4, deq_g=deq_g,
             yd=yd, outT=outT,
             xno_d=xno_d, cgo_d=cgo_d, xo_d=xo_d,
             deq_h=deq.tensor, deqg_h=deq_g.tensor,
             packed={0: gp, 1: upk, 2: dp}, scalesd={0: gs, 1: us, 2: ds})

    sbm = {}
    for name, (off, shape, dtype) in LAYOUT.items():
        sbm[name] = nc.alloc_sbuf_tensor_at(name, shape, dtype,
                                            offset=MAPB + off)
    T["sb"] = sbm

    es = ExitStack()
    try:
        sem = {}
        for name in ["sy", "sq", "gq", "pe", "act", "dve", "pool", "cc"]:
            sem[name] = es.enter_context(nc.semaphore(name))
        T["sem"] = sem
        off_reg = {}
        ctl = dict(nc=nc, sem=sem, ck=ck, inc=inc, val=val, off_reg=off_reg,
                   stage=stage)

        _emit_setup(ctl, T)
        _emit_ln(ctl, T)
        if stage <= 1:
            ctl["pp"].close()
            _final(nc, sem, ck, "sq", "ln_done")
            return nc, ck
        _emit_qkv(ctl, T)
        if stage <= 2:
            _final(nc, sem, ck, "sq", "dbg2")
            return nc, ck
        _emit_scores(ctl, T)
        if stage <= 3:
            _final(nc, sem, ck, "sy", "ctx_done")
            return nc, ck
        _emit_decode(ctl, T)
        if not os.environ.get("K_ONLY_DECODE"):
            _emit_ags(ctl, T)
            if not os.environ.get("K_NO_WO"):
                _emit_wo(ctl, T)
        if stage <= 4:
            _final(nc, sem, ck, "sq", "s_done")
            return nc, ck
        if stage == 5:
            _final(nc, sem, ck, "sy", "deq_wr_done")
            return nc, ck
        _emit_ffn(ctl, T)
        if stage == 6:
            ctl["ffn_pp"].close()
            _final(nc, sem, ck, "gq", "y_done")
            return nc, ck
        _emit_epilogue(ctl, T)
        _final(nc, sem, ck, "sq", "out_done")
        return nc, ck
    finally:
        es.close()


def _emit_setup(ctl, T):
    nc, sem, ck, inc, val = ctl["nc"], ctl["sem"], ctl["ck"], ctl["inc"], ctl["val"]
    sb = T["sb"]
    with nc.Block() as b:
        @b.sync
        def _(sy):
            for dst, src in (("off_s", T["offi"]), ("mtri_s", T["mtri"]),
                             ("psw_s", T["psw"]), ("idn_s", T["idn"]),
                             ("bo_s", T["bo"])):
                sy.dma_start(out=sb[dst][:], in_=src[:]).then_inc(sem["sy"], 16)
                inc("sy", 16)
            sy.wait_ge(sem["sy"], val("sy"))

        @b.vector
        def _(v):
            v.memset(sb["ones_r"][:], 1.0)
            v.memset(sb["c_m30"][:], -30.0)
            v.memset(sb["ones_c"][:], 1.0).then_inc(sem["dve"], 1)
            inc("dve")
            ck.put("ones", val("dve"))
    nc.const_aps.register(F32, -30.0, sb["c_m30"][:])


def _emit_ln(ctl, T):
    nc, sem, ck, inc, val = ctl["nc"], ctl["sem"], ctl["ck"], ctl["inc"], ctl["val"]
    sb = T["sb"]
    xT, xn_d = T["xT"], T["xn_d"]
    x_st, sq_st, t_st, xnst = sb["x_st"], sb["sq_st"], sb["t_st"], sb["xnst"]
    m_row, v_row, r_row = sb["m_row"], sb["v_row"], sb["r_row"]
    ones_r, ones_c = sb["ones_r"], sb["ones_c"]
    TT = 256
    NTT = int(os.environ.get("K_NTT", S // TT))
    pp = ExitStack()
    ln_s1 = pp.enter_context(nc.psum_tensor("ln_s1", [1, TT], F32))
    ln_s2 = pp.enter_context(nc.psum_tensor("ln_s2", [1, TT], F32))
    ln_bc = pp.enter_context(nc.psum_tensor("ln_bc", [128, 2 * TT], F32))
    qkv_p0 = pp.enter_context(nc.psum_tensor("qkv_p0", [128, 512], F32))
    qkv_p1 = pp.enter_context(nc.psum_tensor("qkv_p1", [128, 512], F32))
    sw_p = pp.enter_context(nc.psum_tensor("sw_p", [128, 512], F32))
    vt_p = pp.enter_context(nc.psum_tensor("vt_p", [128, 512], BF16))
    ctl["pp"] = pp
    ctl["psums"] = dict(qkv_p0=qkv_p0, qkv_p1=qkv_p1, sw_p=sw_p, vt_p=vt_p)

    with nc.Block() as b:
        @b.sync
        def _(sy):
            for tt in range(NTT):
                if tt >= 1:
                    sy.wait_ge(sem["dve"], ck.get(f"ln_norm{tt - 1}"))
                sy.dma_start(
                    out=x_st[:].rearrange("p (kc t) -> p kc t", t=TT),
                    in_=xT[:, tt * TT:(tt + 1) * TT].rearrange(
                        "(kc p) t -> p kc t", p=128),
                ).then_inc(sem["sy"], 16)
                inc("sy", 16)
                ck.put(f"ln_x{tt}", val("sy"))

        def emit_store(sc, tt):
            sc.wait_ge(sem["dve"], ck.get(f"ln_norm{tt}"))
            sc.dma_start(
                out=xn_d[:, tt * TT:(tt + 1) * TT].rearrange(
                    "(kc p) t -> p kc t", p=128),
                in_=xnst[:].rearrange("p (kc t) -> p kc t", t=TT),
            ).then_inc(sem["sq"], 16)
            inc("sq", 16)
            ck.put(f"ln_st{tt}", val("sq"))

        @b.scalar
        def _(sc):
            sc.preload_act_table(AF.Sqrt)
            for tt in range(NTT):
                sc.wait_ge(sem["sy"], ck.get(f"ln_x{tt}"))
                for kc in range(16):
                    sc.wait_ge(sem["pe"], ck.get(f"ln_sqfree_{tt}_{kc}"))
                    sc.activation(sq_st[:], x_st[:, kc * TT:(kc + 1) * TT],
                                  AF.Square).then_inc(sem["act"], 1)
                    inc("act")
                    ck.put(f"ln_sq_{tt}_{kc}", val("act"))
                sc.wait_ge(sem["dve"], ck.get(f"ln_var{tt}"))
                sc.activation(r_row[:], v_row[:],
                              AF.Sqrt).then_inc(sem["act"], 1)
                inc("act")
                ck.put(f"ln_sqrt{tt}", val("act"))
                if tt >= 1:
                    emit_store(sc, tt - 1)
            emit_store(sc, NTT - 1)
            ck.put("ln_done", val("sq"))

        @b.tensor
        def _(t):
            for tt in range(NTT):
                t.wait_ge(sem["sy"], ck.get(f"ln_x{tt}"))
                if tt >= 1:
                    t.wait_ge(sem["dve"], ck.get(f"ln_stat{tt - 1}"))
                for kc in range(16):
                    t.matmul(ln_s1[:], ones_c[:], x_st[:, kc * TT:(kc + 1) * TT],
                             start=(kc == 0), stop=(kc == 15))
                for kc in range(16):
                    t.wait_ge(sem["act"], ck.get(f"ln_sq_{tt}_{kc}"))
                    t.matmul(ln_s2[:], ones_c[:], sq_st[:],
                             start=(kc == 0), stop=(kc == 15)).then_inc(sem["pe"], 1)
                    inc("pe")
                    if kc < 15:
                        ck.put(f"ln_sqfree_{tt}_{kc + 1}", val("pe"))
                    elif tt < NTT - 1:
                        ck.put(f"ln_sqfree_{tt + 1}_0", val("pe"))
                ck.put(f"ln_mm2_{tt}", val("pe"))
                t.wait_ge(sem["dve"], ck.get(f"ln_rstd{tt}"))
                if tt >= 1:
                    t.wait_ge(sem["dve"], ck.get(f"ln_norm{tt - 1}"))
                t.matmul(ln_bc[:, 0:TT], ones_r[:], m_row[:],
                         start=True, stop=True)
                t.matmul(ln_bc[:, TT:2 * TT], ones_r[:], r_row[:],
                         start=True, stop=True).then_inc(sem["pe"], 1)
                inc("pe")
                ck.put(f"ln_bc{tt}", val("pe"))

        @b.vector
        def _(v):
            for tt in range(NTT):
                v.wait_ge(sem["pe"], ck.get(f"ln_mm2_{tt}"))
                v.tensor_scalar(m_row[:], ln_s1[:], 1.0 / D, None, ALU.mult)
                v.tensor_scalar(v_row[:], ln_s2[:], 1.0 / D, 1e-5,
                                ALU.mult, ALU.add)
                v.tensor_tensor(t_st[0:1, 0:TT], m_row[:], m_row[:], ALU.mult)
                v.tensor_tensor(v_row[:], v_row[:], t_st[0:1, 0:TT],
                                ALU.subtract).then_inc(sem["dve"], 1)
                inc("dve")
                ck.put(f"ln_var{tt}", val("dve"))
                ck.put(f"ln_stat{tt}", val("dve"))
                v.wait_ge(sem["act"], ck.get(f"ln_sqrt{tt}"))
                v.reciprocal(r_row[:], r_row[:]).then_inc(sem["dve"], 1)
                inc("dve")
                ck.put(f"ln_rstd{tt}", val("dve"))
                v.wait_ge(sem["pe"], ck.get(f"ln_bc{tt}"))
                if tt >= 1:
                    v.wait_ge(sem["sq"], ck.get(f"ln_st{tt - 1}"))
                for kc in range(16):
                    v.tensor_tensor(t_st[:], x_st[:, kc * TT:(kc + 1) * TT],
                                    ln_bc[:, 0:TT], ALU.subtract)
                    v.tensor_tensor(xnst[:, kc * TT:(kc + 1) * TT], t_st[:],
                                    ln_bc[:, TT:2 * TT], ALU.mult)
                v.engine_nop().then_inc(sem["dve"], 1)
                inc("dve")
                ck.put(f"ln_norm{tt}", val("dve"))



def _emit_qkv(ctl, T):
    nc, sem, ck, inc, val = ctl["nc"], ctl["sem"], ctl["ck"], ctl["inc"], ctl["val"]
    stage = ctl["stage"]
    sb = T["sb"]
    xn_d = T["xn_d"]
    ps = ctl["psums"]
    qkv_p = [ps["qkv_p0"], ps["qkv_p1"]]
    sw_p, vt_p = ps["sw_p"], ps["vt_p"]
    wqm, xn_st = sb["wqm"], sb["xn_st"]
    k_raw, v_raw = sb["k_raw"], sb["v_raw"]
    rc_s, rs_s, ro1, ro2 = sb["rc_s"], sb["rs_s"], sb["ro1"], sb["ro2"]
    q_raw = [sb[f"qr{m}"] for m in range(8)]
    kt = [sb[f"kt{g}"] for g in range(NG)]
    v_tm = sb["v_tm"]
    psw_s, idn_s = sb["psw_s"], sb["idn_s"]

    groups = [("q", m) for m in range(8)] + [("k", m) for m in range(2)] + \
             [("v", m) for m in range(2)]
    NTQ = int(os.environ.get("K_NTQ", 4))
    NGRP = int(os.environ.get("K_NGRP", 12))

    with nc.Block() as b:
        @b.sync
        def _(sy):
            sy.dma_start(out=rc_s[:], in_=T["rcos"][:]).then_inc(sem["sy"], 16)
            inc("sy", 16)
            sy.dma_start(out=rs_s[:], in_=T["rsin"][:]).then_inc(sem["sy"], 16)
            inc("sy", 16)
            ck.put("rope_t", val("sy"))
            for tq in range(NTQ):
                sy.wait_ge(sem["sq"], ck.get(f"ln_st{2 * tq + 1}"))
                if tq >= 1:
                    sy.wait_ge(sem["pe"], ck.get(f"qkv_tqdone{tq - 1}"))
                sy.dma_start(
                    out=xn_st[:].rearrange("p (kc t) -> p kc t", t=512),
                    in_=xn_d[:, tq * 512:(tq + 1) * 512].rearrange(
                        "(kc p) t -> p kc t", p=128),
                ).then_inc(sem["sy"], 16)
                inc("sy", 16)
                ck.put(f"qkv_xn{tq}", val("sy"))

        @b.gpsimd
        def _(g):
            for tq in range(NTQ):
                for gi, (nm, m) in enumerate(groups[:NGRP]):
                    gg = tq * 12 + gi
                    if gg >= 2:
                        g.wait_ge(sem["pe"], ck.get(f"qkv_mm_g{gg - 2}"))
                    src = {"q": T["wq"], "k": T["wk"], "v": T["wv"]}[nm]
                    g.dma_start(
                        out=wqm[:, (gg % 2) * 2048:(gg % 2) * 2048 + 2048]
                        .rearrange("p (kc c) -> p kc c", c=128),
                        in_=src[:, m * 128:(m + 1) * 128].rearrange(
                            "(kc p) c -> p kc c", p=128),
                    ).then_inc(sem["gq"], 16)
                    inc("gq", 16)
                    ck.put(f"w_g{gg}", val("gq"))

        @b.tensor
        def _(t):
            for tq in range(NTQ):
                t.wait_ge(sem["sy"], ck.get(f"qkv_xn{tq}"))
                for gi, (nm, m) in enumerate(groups[:NGRP]):
                    gg = tq * 12 + gi
                    t.wait_ge(sem["gq"], ck.get(f"w_g{gg}"))
                    t.wait_ge(sem["dve"], ck.get(f"qkv_pfree_g{gg}"))
                    for kc in range(16):
                        mm = t.matmul(qkv_p[gg % 2][:],
                                      wqm[:, (gg % 2) * 2048 + kc * 128:
                                          (gg % 2) * 2048 + kc * 128 + 128],
                                      xn_st[:, kc * 512:(kc + 1) * 512],
                                      start=(kc == 0), stop=(kc == 15))
                    mm.then_inc(sem["pe"], 1)
                    inc("pe")
                    ck.put(f"qkv_mm_g{gg}", val("pe"))
                ck.put(f"qkv_tqdone{tq}", val("pe"))
            swlist = [("q", m, tq) for m in range(8) for tq in range(4)] + \
                     [("k", m, tq) for m in range(2) for tq in range(4)]
            for si, (nm, m, tq) in enumerate(swlist):
                t.wait_ge(sem["dve"], ck.get(f"ev_{nm}{m}_{tq}"))
                if si >= 1:
                    t.wait_ge(sem["dve"], ck.get(f"rope_s{si - 1}"))
                srcap = (q_raw[m][:, tq * 512:(tq + 1) * 512] if nm == "q"
                         else k_raw[:, m * S + tq * 512:m * S + tq * 512 + 512])
                t.matmul(sw_p[:], psw_s[:], srcap,
                         start=True, stop=True).then_inc(sem["pe"], 1)
                inc("pe")
                ck.put(f"sw_s{si}", val("pe"))
            for g_ in range(NG):
                for kc in range(16):
                    t.wait_ge(sem["dve"], ck.get(f"ev_v{g_ // 2}_{kc // 4}"))
                    t.wait_ge(sem["dve"], ck.get(f"vt_free_{g_}_{kc // 4}"))
                    tr = t.transpose(vt_p[:, (kc % 4) * 128:(kc % 4) * 128 + 64],
                                     v_raw[(g_ % 2) * 64:(g_ % 2) * 64 + 64,
                                           (g_ // 2) * S + kc * 128:
                                           (g_ // 2) * S + kc * 128 + 128],
                                     idn_s[(g_ % 2) * 64:(g_ % 2) * 64 + 64,
                                           (g_ % 2) * 64:(g_ % 2) * 64 + 64])
                    if kc % 4 == 3:
                        tr.then_inc(sem["pe"], 1)
                        inc("pe")
                        ck.put(f"vt_{g_}_{kc // 4}", val("pe"))


        @b.vector
        def _(v):
            for tq in range(NTQ):
                for gi, (nm, m) in enumerate(groups[:NGRP]):
                    gg = tq * 12 + gi
                    v.wait_ge(sem["pe"], ck.get(f"qkv_mm_g{gg}"))
                    dst = (q_raw[m][:, tq * 512:(tq + 1) * 512] if nm == "q"
                           else (k_raw if nm == "k" else v_raw)
                           [:, m * S + tq * 512:m * S + tq * 512 + 512])
                    if os.environ.get("K_NO_EVAC"):
                        v.engine_nop().then_inc(sem["dve"], 1)
                    else:
                        v.tensor_copy(dst, qkv_p[gg % 2][:]).then_inc(sem["dve"], 1)
                    inc("dve")
                    ck.put(f"ev_{nm}{m}_{tq}", val("dve"))
                    ck.put(f"qkv_pfree_g{gg + 2}", val("dve"))
            v.wait_ge(sem["sy"], ck.get("rope_t"))
            swlist = [("q", m, tq) for m in range(8) for tq in range(4)] + \
                     [("k", m, tq) for m in range(2) for tq in range(4)]
            for si, (nm, m, tq) in enumerate(swlist):
                v.wait_ge(sem["pe"], ck.get(f"sw_s{si}"))
                if nm == "q":
                    dst = q_raw[m][:, tq * 512:(tq + 1) * 512]
                else:
                    dst = k_raw[:, m * S + tq * 512:m * S + tq * 512 + 512]
                csl = slice(tq * 512, tq * 512 + 512)
                if os.environ.get("K_NO_ROPE"):
                    v.engine_nop().then_inc(sem["dve"], 1)
                else:
                    v.tensor_tensor(ro1[:], dst, rc_s[:, csl], ALU.mult)
                    v.tensor_tensor(ro2[:], sw_p[:], rs_s[:, csl], ALU.mult)
                    v.tensor_tensor(dst, ro1[:], ro2[:],
                                    ALU.add).then_inc(sem["dve"], 1)
                inc("dve")
                ck.put(f"rope_s{si}", val("dve"))
                ck.put(f"rope_{nm}{m}_{tq}", val("dve"))
            for g_ in range(NG):
                v.wait_ge(sem["dve"], ck.get(f"rope_k{g_ // 2}_3"))
                if os.environ.get("K_NO_KT"):
                    continue
                v.tensor_copy(kt[g_][0:64, :],
                              k_raw[(g_ % 2) * 64:(g_ % 2) * 64 + 64,
                                    (g_ // 2) * S:(g_ // 2) * S + S])
            v.engine_nop().then_inc(sem["dve"], 1)
            inc("dve")
            ck.put("kt_done", val("dve"))
            if not os.environ.get("K_NO_MEMSET_VTM"):
                v.memset(v_tm[:].rearrange("p (b s) -> p b s", s=66)[:, :, 64:65],
                         1.0)
            for g_ in range(NG):
                for kq in range(4):
                    v.wait_ge(sem["pe"], ck.get(f"vt_{g_}_{kq}"))
                    for kk in range(4):
                        kc = kq * 4 + kk
                        v.tensor_copy(
                            v_tm[:, (g_ * 16 + kc) * 66:(g_ * 16 + kc) * 66 + 64],
                            vt_p[:, kk * 128:kk * 128 + 64])
                    v.engine_nop().then_inc(sem["dve"], 1)
                    inc("dve")
                    nxt = (g_, kq + 1) if kq < 3 else (g_ + 1, 0)
                    if nxt[0] < NG:
                        ck.put(f"vt_free_{nxt[0]}_{nxt[1]}", val("dve"))
            v.engine_nop().then_inc(sem["dve"], 1)
            inc("dve")
            ck.put("vtm_done", val("dve"))

        @b.scalar
        def _(sc):
            sc.wait_ge(sem["dve"], ck.get("kt_done"))
            for g_ in range(NG):
                if os.environ.get("K_NO_KT"):
                    sc.dma_start(out=kt[g_][0:6, :],
                                 in_=T["ka"][:]).then_inc(sem["sq"], 16)
                else:
                    sc.dma_start(out=kt[g_][64:70, :],
                                 in_=T["ka"][:]).then_inc(sem["sq"], 16)
                inc("sq", 16)
            ck.put("ka_done", val("sq"))
            if stage == 2:
                sc.wait_ge(sem["dve"], ck.get("vtm_done"))
                for m in range(8):
                    sc.dma_start(out=T["q_d"][m * 128:(m + 1) * 128, :],
                                 in_=q_raw[m][:]).then_inc(sem["sq"], 16)
                    inc("sq", 16)
                for m in range(2):
                    sc.dma_start(out=T["k_d"][m * 128:(m + 1) * 128, :],
                                 in_=k_raw[:, m * S:(m + 1) * S]
                                 ).then_inc(sem["sq"], 16)
                    inc("sq", 16)
                    sc.dma_start(out=T["v_d"][m * 128:(m + 1) * 128, :],
                                 in_=v_raw[:, m * S:(m + 1) * S]
                                 ).then_inc(sem["sq"], 16)
                    inc("sq", 16)
                ck.put("dbg2", val("sq"))

    ctl["pp"].close()


def _emit_scores(ctl, T):
    nc, sem, ck, inc, val = ctl["nc"], ctl["sem"], ctl["ck"], ctl["inc"], ctl["val"]
    sb = T["sb"]
    qt = [sb["qt0"], sb["qt1"]]
    ss = [sb["ss0"], sb["ss1"]]
    pb = [sb["pb0"], sb["pb1"]]
    rr_s, rb_s, ctxst = sb["rr_s"], sb["rb_s"], sb["ctxst"]
    kt = [sb[f"kt{g}"] for g in range(NG)]
    v_tm = sb["v_tm"]
    q_raw = [sb[f"qr{m}"] for m in range(8)]
    mtri_s, ones_r = sb["mtri_s"], sb["ones_r"]

    pp = ExitStack()
    sc_A = pp.enter_context(nc.psum_tensor("sc_A", [128, 1024], F32))
    sc_B = pp.enter_context(nc.psum_tensor("sc_B", [128, 1024], F32))
    ctx_p = pp.enter_context(nc.psum_tensor("ctx_p", [65, S], F32))
    scp = [sc_A, sc_B]

    with nc.Block() as b:
        def emit_ctxw(sy, hp):
            sy.wait_ge(sem["dve"], ck.get(f"ctxn{hp}"))
            sy.dma_start(out=T["ctx_d"][hp * 64:(hp + 1) * 64, :],
                         in_=ctxst[(hp % 2) * 64:(hp % 2) * 64 + 64, :]
                         ).then_inc(sem["sy"], 16)
            inc("sy", 16)
            ck.put(f"ctxw{hp}", val("sy"))

        @b.sync
        def _(sy):
            sy.wait_ge(sem["dve"], ck.get("vtm_done"))
            for hp in range(NP):
                par = hp % 2
                if hp >= 2:
                    sy.wait_ge(sem["pe"], ck.get(f"pv_stop{hp - 2}"))
                sy.dma_start(out=qt[par][64:70, :],
                             in_=T["qa"][hp]).then_inc(sem["sy"], 16)
                inc("sy", 16)
                ck.put(f"qa{hp}", val("sy"))
                if hp >= 2:
                    emit_ctxw(sy, hp - 2)
            emit_ctxw(sy, NP - 2)
            emit_ctxw(sy, NP - 1)
            ck.put("ctx_done", val("sy"))

        @b.vector
        def _(v):
            v.wait_ge(sem["dve"], ck.get("vtm_done"))
            for hp in range(NP):
                par = hp % 2
                if hp >= 2:
                    v.wait_ge(sem["pe"], ck.get(f"pv_stop{hp - 2}"))
                v.tensor_copy(qt[par][0:64, :],
                              q_raw[hp // 2][(hp % 2) * 64:(hp % 2) * 64 + 64, :]
                              ).then_inc(sem["dve"], 1)
                inc("dve")
                ck.put(f"qtc{hp}", val("dve"))
                np_ = 0
                for kc in range(16):
                    sp = np_ % 2
                    v.wait_ge(sem["act"], ck.get(f"tanh_{hp}_{np_}"))
                    v.tensor_tensor(ss[sp][:, 0:128], ss[sp][:, 0:128],
                                    mtri_s[:], ALU.add).then_inc(sem["dve"], 1)
                    inc("dve")
                    ck.put(f"mask_{hp}_{kc}", val("dve"))
                    np_ += len(pieces(kc))
                v.wait_ge(sem["pe"], ck.get(f"pv_stop{hp}"))
                v.reciprocal(rr_s[:], ctx_p[64:65, :]).then_inc(sem["dve"], 1)
                inc("dve")
                ck.put(f"recip{hp}", val("dve"))
                v.wait_ge(sem["pe"], ck.get(f"bc{hp}"))
                v.tensor_copy(rb_s[:, 0:1024], sc_A[0:64, :])
                v.tensor_copy(rb_s[:, 1024:2048], sc_B[0:64, :])
                if hp >= 2:
                    v.wait_ge(sem["sy"], ck.get(f"ctxw{hp - 2}"))
                v.tensor_tensor(
                    ctxst[(hp % 2) * 64:(hp % 2) * 64 + 64, :],
                    ctx_p[0:64, :], rb_s[:], ALU.mult).then_inc(sem["dve"], 1)
                inc("dve")
                ck.put(f"ctxn{hp}", val("dve"))

        @b.tensor
        def _(t):
            t.wait_ge(sem["sq"], ck.get("ka_done"))
            for hp in range(NP):
                par = hp % 2
                gl = hp // 4
                t.wait_ge(sem["dve"], ck.get(f"qtc{hp}"))
                t.wait_ge(sem["sy"], ck.get(f"qa{hp}"))
                if hp >= 1:
                    t.wait_ge(sem["dve"], ck.get(f"ctxn{hp - 1}"))
                np_ = 0
                for kc in range(16):
                    for pi, (pc, pe_) in enumerate(pieces(kc)):
                        sp = np_ % 2
                        t.wait_ge(sem["act"], ck.get(f"scfree_{hp}_{np_}"))
                        cuts = list(range(0, pe_ - pc, 512)) + [pe_ - pc]
                        for ci in range(len(cuts) - 1):
                            c0, c1 = cuts[ci], cuts[ci + 1]
                            mm = t.matmul(scp[sp][0:128, c0:c1],
                                          kt[gl][0:KEXT, kc * 128:kc * 128 + 128],
                                          qt[par][0:KEXT, pc + c0:pc + c1],
                                          start=True, stop=True)
                        mm.then_inc(sem["pe"], 1)
                        inc("pe")
                        ck.put(f"sc_mm_{hp}_{np_}", val("pe"))
                        t.wait_ge(sem["act"], ck.get(f"exp_{hp}_{np_}"))
                        acuts = sorted({pc, pe_} |
                                       set(range((pc + 511) // 512 * 512,
                                                 pe_, 512)))
                        for ci in range(len(acuts) - 1):
                            a0, a1 = acuts[ci], acuts[ci + 1]
                            mm = t.matmul(ctx_p[0:65, a0:a1],
                                          v_tm[:, (gl * 16 + kc) * 66:
                                               (gl * 16 + kc) * 66 + 65],
                                          pb[sp][0:128, a0 - pc:a1 - pc],
                                          start=(kc == 0), stop=(kc == 15),
                                          skip_group_check=True)
                        mm.then_inc(sem["pe"], 1)
                        inc("pe")
                        ck.put(f"pv_{hp}_{np_}", val("pe"))
                        np_ += 1
                ck.put(f"pv_stop{hp}", val("pe"))
                t.wait_ge(sem["dve"], ck.get(f"recip{hp}"))
                for j in range(2):
                    t.matmul(sc_A[0:64, j * 512:(j + 1) * 512],
                             ones_r[0:1, 0:64],
                             rr_s[0:1, j * 512:(j + 1) * 512],
                             start=True, stop=True)
                for j in range(2):
                    mm = t.matmul(sc_B[0:64, j * 512:(j + 1) * 512],
                                  ones_r[0:1, 0:64],
                                  rr_s[0:1, 1024 + j * 512:1024 + (j + 1) * 512],
                                  start=True, stop=True)
                mm.then_inc(sem["pe"], 1)
                inc("pe")
                ck.put(f"bc{hp}", val("pe"))

        @b.scalar
        def _(sc):
            sc.preload_act_table(AF.Exp)
            for hp in range(NP):
                np_ = 0
                for kc in range(16):
                    for pi, (pc, pe_) in enumerate(pieces(kc)):
                        sp = np_ % 2
                        n = pe_ - pc
                        sc.wait_ge(sem["pe"], ck.get(f"sc_mm_{hp}_{np_}"))
                        sc.activation(ss[sp][:, 0:n], scp[sp][0:128, 0:n],
                                      AF.Tanh, scale=1.0 / 240.0
                                      ).then_inc(sem["act"], 1)
                        inc("act")
                        ck.put(f"tanh_{hp}_{np_}", val("act"))
                        k2 = np_ + 2
                        if k2 < NPIECES:
                            ck.put(f"scfree_{hp}_{k2}", val("act"))
                        elif hp + 1 < NP:
                            ck.put(f"scfree_{hp + 1}_{k2 - NPIECES}", val("act"))
                        if pi == 0:
                            sc.wait_ge(sem["dve"], ck.get(f"mask_{hp}_{kc}"))
                        if np_ >= 2:
                            sc.wait_ge(sem["pe"], ck.get(f"pv_{hp}_{np_ - 2}"))
                        elif hp >= 1:
                            sc.wait_ge(sem["pe"],
                                       ck.get(f"pv_{hp - 1}_{NPIECES - 2 + np_}"))
                        sc.activation(pb[sp][:, 0:n], ss[sp][:, 0:n], AF.Exp,
                                      scale=30.0, bias=-30.0
                                      ).then_inc(sem["act"], 1)
                        inc("act")
                        ck.put(f"exp_{hp}_{np_}", val("act"))
                        np_ += 1

    pp.close()


def _emit_decode(ctl, T):
    nc, sem, ck, inc, val = ctl["nc"], ctl["sem"], ctl["ck"], ctl["inc"], ctl["val"]
    sb = T["sb"]
    packed, scalesd, deq_h = T["packed"], T["scalesd"], T["deq_h"]
    pk_b = [sb[f"pkb{i}"] for i in range(4)]
    scl_b = [sb[f"sclb{i}"] for i in range(4)]
    dec_b = [sb[f"dcb{i}"] for i in range(4)]
    dqi0, dqf0, dqg0 = sb["dqi0"], sb["dqf0"], sb["dqg0"]
    trst4 = [sb[f"trstb{i}"] for i in range(4)]
    idn_s = sb["idn_s"]

    def emit_tile(eng, buf):
        pk = pk_b[buf]
        scb = scl_b[buf]
        dec = dec_b[buf]
        i1, f1, f2 = dqi0, dqf0, dqg0
        for h in range(2):
            if h == 0:
                eng.tensor_scalar(i1[:], pk[:], 15, None, ALU.bitwise_and)
            else:
                eng.tensor_scalar(i1[:], pk[:], 4, None, ALU.logical_shift_right)
            eng.tensor_scalar(f1[:].bitcast(I32), i1[:], 7, 22,
                              ALU.bitwise_and, ALU.logical_shift_left)
            eng.tensor_scalar(f1[:].bitcast(I32), f1[:].bitcast(I32),
                              126 << 23, None, ALU.add)
            eng.tensor_scalar(f2[:].bitcast(I32), i1[:], 7, None, ALU.bitwise_and)
            eng.tensor_scalar(f2[:], f2[:].bitcast(I32), -0.25, 0.5,
                              ALU.mult, ALU.add)
            eng.tensor_scalar(f2[:], f2[:], 0.0, None, ALU.max)
            eng.tensor_scalar(i1[:], i1[:], 8, None, ALU.bitwise_and)
            eng.tensor_scalar(i1[:].bitcast(F32), i1[:], -0.25, 1.0,
                              ALU.mult, ALU.add)
            eng.tensor_tensor(f2[:], f1[:], f2[:], ALU.subtract)
            eng.tensor_tensor(f2[:], f2[:], i1[:].bitcast(F32), ALU.mult)
            sv = scb[:].unsqueeze(2).broadcast_to([128, 64, 16])
            last = eng.tensor_tensor(
                dec[:, h * 1024:(h + 1) * 1024].rearrange(
                    "p (bk r) -> p bk r", r=16),
                f2[:].rearrange("p (bk r) -> p bk r", r=16),
                sv, ALU.mult)
        return last

    pp = ExitStack()
    tp_p = pp.enter_context(nc.psum_tensor("tp_p", [128, 2048], BF16))

    with nc.Block() as b:
        @b.gpsimd
        def _(g):
            for i in range(48):
                w, ti = i // 16, i % 16
                if i >= 4:
                    g.wait_ge(sem["dve"], ck.get(f"dec{i - 4}"))
                g.dma_start(out=pk_b[i % 4][:],
                            in_=packed[w][ti * 128:(ti + 1) * 128, :]
                            ).then_inc(sem["gq"], 16)
                inc("gq", 16)
                g.dma_start(out=scl_b[i % 4][:],
                            in_=scalesd[w][ti * 128:(ti + 1) * 128, :]
                            ).then_inc(sem["gq"], 16)
                inc("gq", 16)
                ck.put(f"pk{i}", val("gq"))

        def emit_evac(v, j):
            if j >= 4:
                v.wait_ge(sem["sy"], ck.get(f"dqw{j - 4}"))
            v.wait_ge(sem["pe"], ck.get(f"tr{j}"))
            v.tensor_copy(trst4[j % 4][:], tp_p[:]).then_inc(sem["dve"], 1)
            inc("dve")
            ck.put(f"evac{j}", val("dve"))

        @b.vector
        def _(v):
            for i in range(48):
                v.wait_ge(sem["gq"], ck.get(f"pk{i}"))
                if i >= 4:
                    v.wait_ge(sem["pe"], ck.get(f"tr{i - 4}"))
                emit_tile(v, i % 4).then_inc(sem["dve"], 1)
                inc("dve")
                ck.put(f"dec{i}", val("dve"))
                if i >= 1:
                    emit_evac(v, i - 1)
            emit_evac(v, 47)

        @b.tensor
        def _(t):
            t.wait_ge(sem["dve"], ck.get(f"ctxn{NP - 1}"))
            for i in range(48):
                t.wait_ge(sem["dve"], ck.get(f"dec{i}"))
                if i >= 1:
                    t.wait_ge(sem["dve"], ck.get(f"evac{i - 1}"))
                for kb in range(16):
                    tr = t.transpose(tp_p[:, kb * 128:(kb + 1) * 128],
                                     dec_b[i % 4][:, kb * 128:(kb + 1) * 128],
                                     idn_s[:])
                tr.then_inc(sem["pe"], 1)
                inc("pe")
                ck.put(f"tr{i}", val("pe"))

        @b.sync
        def _(sy):
            for i in range(48):
                w, ti = i // 16, i % 16
                sy.wait_ge(sem["dve"], ck.get(f"evac{i}"))
                if w < 2:
                    base = w * WSLICE + ti * 16 * 16384
                    dims = [[128, 128], [16384, 16], [1, 128]]
                else:
                    base = 2 * WSLICE + ti * 16384
                    dims = [[128, 128], [16 * 16384, 16], [1, 128]]
                sy.dma_start(
                    out=bass.AP(deq_h, base, dims),
                    in_=trst4[i % 4][:].rearrange("p (kb n) -> p kb n", n=128),
                ).then_inc(sem["sy"], 16)
                inc("sy", 16)
                ck.put(f"dqw{i}", val("sy"))
            ck.put("deq_wr_done", val("sy"))

    pp.close()


def _emit_ags(ctl, T):
    nc, sem, ck, inc, val = ctl["nc"], ctl["sem"], ctl["ck"], ctl["inc"], ctl["val"]
    with nc.Block() as b:
        @b.gpsimd
        def _(g):
            g.wait_ge(sem["sy"], ck.get("ctx_done"))
            g.collective_compute(
                "AllGather", ALU.bypass,
                replica_groups=[[0, 1], [2, 3], [4, 5], [6, 7]],
                ins=[T["ctx_d"][:]], outs=[T["ctx_g"][:]],
            ).then_inc(sem["cc"], 1)
            inc("cc")
            ck.put("ctx_ag", val("cc"))
            if ctl["stage"] != 5 and not os.environ.get("K_NO_DEQAG"):
                g.wait_ge(sem["sy"], ck.get("deq_wr_done"))
                rounds = [
                    ([[0, 1], [2, 3], [4, 5], [6, 7]], T["deq"], T["deq_g2"]),
                    ([[0, 2], [1, 3], [4, 6], [5, 7]], T["deq_g2"], T["deq_g4"]),
                    ([[0, 4], [1, 5], [2, 6], [3, 7]], T["deq_g4"], T["deq_g"]),
                ]
                for rg, src_t, dst_t in rounds:
                    g.collective_compute(
                        "AllGather", ALU.bypass,
                        replica_groups=rg,
                        ins=[src_t[:]], outs=[dst_t[:]],
                    ).then_inc(sem["cc"], 1)
                    inc("cc")
                    g.wait_ge(sem["cc"], val("cc"))
                ck.put("w_ag", val("cc"))


def _emit_wo(ctl, T):
    nc, sem, ck, inc, val = ctl["nc"], ctl["sem"], ctl["ck"], ctl["inc"], ctl["val"]
    sb = T["sb"]
    cg_t, wom, s_stage = sb["cg_t"], sb["wom"], sb["s_stage"]
    bo_s = sb["bo_s"]

    pp = ExitStack()
    s_p = [pp.enter_context(nc.psum_tensor(f"s_p{i}", [128, 1024], F32))
           for i in range(2)]

    with nc.Block() as b:
        @b.gpsimd
        def _(g):
            g.wait_ge(sem["cc"], ck.get("ctx_ag"))
            offr = g.value_load(T["sb"]["off_s"][0:1, 0:1])
            g.dma_start(out=T["cgo_d"][:],
                        in_=T["ctx_g"][:, bass.ds(offr, TOK)]
                        ).then_inc(sem["gq"], 16)
            inc("gq", 16)
            ck.put("cgo_cp", val("gq"))
            g.wait_ge(sem["dve"], ck.get("dec47"))
            for mb in range(8):
                if mb >= 2:
                    g.wait_ge(sem["pe"], ck.get(f"wo_mm_{2 * mb - 3}"))
                g.dma_start(
                    out=wom[:, (mb % 2) * 4096:(mb % 2) * 4096 + 4096]
                    .rearrange("p (kc c) -> p kc c", c=256),
                    in_=T["wo"][:, mb * 256:(mb + 1) * 256].rearrange(
                        "(kc p) c -> p kc c", p=128),
                ).then_inc(sem["gq"], 16)
                inc("gq", 16)
                ck.put(f"wo{mb}", val("gq"))

        @b.sync
        def _(sy):
            sy.wait_ge(sem["gq"], ck.get("cgo_cp"))
            sy.dma_start(
                out=cg_t[:].rearrange("p (kc t) -> p kc t", t=1024),
                in_=T["cgo_d"][:].rearrange("(kc p) t -> p kc t", p=128),
            ).then_inc(sem["sy"], 16)
            inc("sy", 16)
            ck.put("cg_ld", val("sy"))

        @b.tensor
        def _(t):
            t.wait_ge(sem["sy"], ck.get("cg_ld"))
            t.wait_ge(sem["dve"], ck.get("evac47"))
            for m in range(16):
                t.wait_ge(sem["gq"], ck.get(f"wo{m // 2}"))
                if m >= 2:
                    t.wait_ge(sem["dve"], ck.get(f"s_ev{m - 2}"))
                base = ((m // 2) % 2) * 4096 + (m % 2) * 128
                for half in range(2):
                    for kc in range(16):
                        mm = t.matmul(
                            s_p[m % 2][:, half * 512:half * 512 + 512],
                            wom[:, base + kc * 256:base + kc * 256 + 128],
                            cg_t[:, kc * 1024 + half * 512:
                                 kc * 1024 + half * 512 + 512],
                            start=(kc == 0), stop=(kc == 15))
                mm.then_inc(sem["pe"], 1)
                inc("pe")
                ck.put(f"wo_mm_{m}", val("pe"))

        @b.vector
        def _(v):
            for m in range(16):
                v.wait_ge(sem["pe"], ck.get(f"wo_mm_{m}"))
                if m >= 2:
                    v.wait_ge(sem["sq"], ck.get(f"s_st{m - 2}"))
                v.tensor_scalar(s_stage[:, (m % 2) * 1024:(m % 2) * 1024 + 1024],
                                s_p[m % 2][:], bo_s[:, m:m + 1], None,
                                ALU.add).then_inc(sem["dve"], 1)
                inc("dve")
                ck.put(f"s_ev{m}", val("dve"))

        @b.scalar
        def _(sc):
            for m in range(16):
                sc.wait_ge(sem["dve"], ck.get(f"s_ev{m}"))
                sc.dma_start(out=T["s_d"][m * 128:(m + 1) * 128, :],
                             in_=s_stage[:, (m % 2) * 1024:(m % 2) * 1024 + 1024]
                             ).then_inc(sem["sq"], 16)
                inc("sq", 16)
                ck.put(f"s_st{m}", val("sq"))
            ck.put("s_done", val("sq"))

    pp.close()


def _emit_ffn(ctl, T):
    nc, sem, ck, inc, val = ctl["nc"], ctl["sem"], ctl["ck"], ctl["inc"], ctl["val"]
    sb = T["sb"]
    deqg_h, yd, xn_d = T["deqg_h"], T["yd"], T["xn_d"]
    xno, itT = sb["xno"], sb["itT"]
    wgb, wub, wdb = sb["wgb"], sb["wub"], sb["wdb"]
    gel, itst, yst = sb["gel"], sb["itst"], sb["yst"]
    idn_s = sb["idn_s"]
    NSPAN = 64
    SUBG = 8
    NSUB = 16

    def nb_addr(nb):
        if nb < 64:
            return nb >> 3, nb & 7
        return (nb - 64) >> 3, 8 + ((nb - 64) & 7)

    pp = ExitStack()
    g_p = pp.enter_context(nc.psum_tensor("g_p", [128, 256], F32))
    u_p = pp.enter_context(nc.psum_tensor("u_p", [128, 256], F32))
    it_p = pp.enter_context(nc.psum_tensor("it_p", [128, 256], BF16))
    y_p = [pp.enter_context(nc.psum_tensor(f"y_p{i}", [128, 1024], F32))
           for i in range(2)]
    ctl["ffn_pp"] = pp

    with nc.Block() as b:
        @b.sync
        def _(sy):
            sy.wait_ge(sem["gq"], ck.get("xno_cp"))
            sy.dma_start(
                out=xno[:].rearrange("p (kc t) -> p kc t", t=1024),
                in_=T["xno_d"][:].rearrange("(kc p) t -> p kc t", p=128),
            ).then_inc(sem["sy"], 16)
            inc("sy", 16)
            ck.put("xno_ld", val("sy"))
            sy.wait_ge(sem["cc"], ck.get("w_ag"))
            for sp_ in range(NSPAN):
                if sp_ >= 2:
                    sy.wait_ge(sem["pe"], ck.get(f"gu_done{sp_ - 2}"))
                n0 = sp_ * 256
                r, nt0 = n0 >> 11, (n0 >> 7) & 15
                for w, buf in ((0, wgb), (1, wub)):
                    for dnt in range(2):
                        base = (r * CSLICE + w * WSLICE +
                                (nt0 + dnt) * 16 * 16384)
                        sy.dma_start(
                            out=buf[:, (sp_ % 2) * 4096:(sp_ % 2) * 4096 + 4096]
                            .rearrange("p (kc nt n) -> p kc nt n",
                                       kc=16, n=128)[:, :, dnt, :],
                            in_=bass.AP(deqg_h, base,
                                        [[128, 128], [16384, 16], [1, 128]]),
                        ).then_inc(sem["sy"], 16)
                        inc("sy", 16)
                ck.put(f"wgu{sp_}", val("sy"))

        @b.tensor
        def _(t):
            t.wait_ge(sem["sy"], ck.get("xno_ld"))
            t.wait_ge(sem["dve"], ck.get("s_ev15"))
            for sg in range(NSUB):
                for s4 in range(4):
                    sp_ = sg * 4 + s4
                    t.wait_ge(sem["sy"], ck.get(f"wgu{sp_}"))
                    for tc in range(8):
                        wb = (sp_ % 2) * 4096
                        t.wait_ge(sem["act"], ck.get(f"gpfree_{sp_}_{tc}"))
                        for kc in range(16):
                            mm = t.matmul(g_p[:],
                                          xno[:, kc * 1024 + tc * 128:
                                              kc * 1024 + tc * 128 + 128],
                                          wgb[:, wb + kc * 256:wb + kc * 256 + 256],
                                          start=(kc == 0), stop=(kc == 15))
                        mm.then_inc(sem["pe"], 1)
                        inc("pe")
                        ck.put(f"gmm_{sp_}_{tc}", val("pe"))
                        t.wait_ge(sem["dve"], ck.get(f"upfree_{sp_}_{tc}"))
                        for kc in range(16):
                            mm = t.matmul(u_p[:],
                                          xno[:, kc * 1024 + tc * 128:
                                              kc * 1024 + tc * 128 + 128],
                                          wub[:, wb + kc * 256:wb + kc * 256 + 256],
                                          start=(kc == 0), stop=(kc == 15))
                        mm.then_inc(sem["pe"], 1)
                        inc("pe")
                        ck.put(f"umm_{sp_}_{tc}", val("pe"))
                        t.wait_ge(sem["dve"], ck.get(f"imul_{sp_}_{tc}"))
                        for bb in range(2):
                            tr = t.transpose(it_p[:, bb * 128:(bb + 1) * 128],
                                             itst[:, (tc % 2) * 256 + bb * 128:
                                                  (tc % 2) * 256 + bb * 128 + 128],
                                             idn_s[:])
                        tr.then_inc(sem["pe"], 1)
                        inc("pe")
                        ck.put(f"itr_{sp_}_{tc}", val("pe"))
                        nxt = (sp_, tc + 2) if tc < 6 else (sp_ + 1, tc - 6)
                        if nxt[0] < NSPAN:
                            ck.put(f"itstfree_{nxt[0]}_{nxt[1]}", val("pe"))
                    ck.put(f"gu_done{sp_}", val("pe"))
                t.wait_ge(sem["sq"], ck.get(f"wd{sg}"))
                t.wait_ge(sem["dve"], ck.get(f"itev_{sg}"))
                for hc in range(16):
                    par = hc % 2
                    t.wait_ge(sem["dve"], ck.get(f"ypfree_{sg}_{hc}"))
                    for half in range(2):
                        for j in range(SUBG):
                            slot = (sg * SUBG + j) % 16
                            mm = t.matmul(
                                y_p[par][:, half * 512:half * 512 + 512],
                                wdb[:, j * 2048 + hc * 128:
                                    j * 2048 + hc * 128 + 128],
                                itT[:, slot * 1024 + half * 512:
                                    slot * 1024 + half * 512 + 512],
                                start=(j == 0), stop=(j == SUBG - 1))
                    mm.then_inc(sem["pe"], 1)
                    inc("pe")
                    ck.put(f"dn_{sg}_{hc}", val("pe"))
                ck.put(f"dn_done{sg}", val("pe"))

        @b.scalar
        def _(sc):
            sc.preload_act_table(AF.Gelu_apprx_tanh)
            for sg in range(NSUB):
                if sg >= 1:
                    sc.wait_ge(sem["pe"], ck.get(f"dn_done{sg - 1}"))
                else:
                    sc.wait_ge(sem["cc"], ck.get("w_ag"))
                for j in range(SUBG):
                    nb = sg * SUBG + j
                    r, nt = nb_addr(nb)
                    base = r * CSLICE + 2 * WSLICE + nt * 16 * 16384
                    sc.dma_start(
                        out=wdb[:, j * 2048:(j + 1) * 2048].rearrange(
                            "p (hcc hh) -> p hcc hh", hh=128),
                        in_=bass.AP(deqg_h, base,
                                    [[128, 128], [16384, 16], [1, 128]]),
                    ).then_inc(sem["sq"], 16)
                    inc("sq", 16)
                ck.put(f"wd{sg}", val("sq"))
                for s4 in range(4):
                    sp_ = sg * 4 + s4
                    for tc in range(8):
                        sc.wait_ge(sem["pe"], ck.get(f"gmm_{sp_}_{tc}"))
                        sc.wait_ge(sem["dve"], ck.get(f"gelfree_{sp_}_{tc}"))
                        sc.activation(gel[:, (tc % 2) * 256:(tc % 2) * 256 + 256],
                                      g_p[:], AF.Gelu_apprx_tanh
                                      ).then_inc(sem["act"], 1)
                        inc("act")
                        ck.put(f"gel_{sp_}_{tc}", val("act"))
                        nxt = (sp_, tc + 1) if tc < 7 else (sp_ + 1, 0)
                        if nxt[0] < NSPAN:
                            ck.put(f"gpfree_{nxt[0]}_{nxt[1]}", val("act"))

        @b.vector
        def _(v):
            for sg in range(NSUB):
                for s4 in range(4):
                    sp_ = sg * 4 + s4
                    if sg >= 2 and s4 == 0:
                        v.wait_ge(sem["pe"], ck.get(f"dn_done{sg - 2}"))
                    for tc in range(8):
                        v.wait_ge(sem["act"], ck.get(f"gel_{sp_}_{tc}"))
                        v.wait_ge(sem["pe"], ck.get(f"umm_{sp_}_{tc}"))
                        v.wait_ge(sem["pe"], ck.get(f"itstfree_{sp_}_{tc}"))
                        v.tensor_tensor(
                            itst[:, (tc % 2) * 256:(tc % 2) * 256 + 256],
                            gel[:, (tc % 2) * 256:(tc % 2) * 256 + 256],
                            u_p[:], ALU.mult).then_inc(sem["dve"], 1)
                        inc("dve")
                        ck.put(f"imul_{sp_}_{tc}", val("dve"))
                        nxtu = (sp_, tc + 1) if tc < 7 else (sp_ + 1, 0)
                        if nxtu[0] < NSPAN:
                            ck.put(f"upfree_{nxtu[0]}_{nxtu[1]}", val("dve"))
                        nxtg = (sp_, tc + 2) if tc < 6 else (sp_ + 1, tc - 6)
                        if nxtg[0] < NSPAN:
                            ck.put(f"gelfree_{nxtg[0]}_{nxtg[1]}", val("dve"))
                        v.wait_ge(sem["pe"], ck.get(f"itr_{sp_}_{tc}"))
                        for bb in range(2):
                            slot = (sp_ * 2 + bb) % 16
                            v.tensor_copy(
                                itT[:, slot * 1024 + tc * 128:
                                    slot * 1024 + tc * 128 + 128],
                                it_p[:, bb * 128:(bb + 1) * 128])
                        v.engine_nop().then_inc(sem["dve"], 1)
                        inc("dve")
                v.engine_nop().then_inc(sem["dve"], 1)
                inc("dve")
                ck.put(f"itev_{sg}", val("dve"))
                for hc in range(16):
                    par = hc % 2
                    v.wait_ge(sem["pe"], ck.get(f"dn_{sg}_{hc}"))
                    v.wait_ge(sem["gq"], ck.get(f"ywfree_{sg}_{hc}"))
                    v.tensor_copy(yst[:, par * 1024:par * 1024 + 1024],
                                  y_p[par][:]).then_inc(sem["dve"], 1)
                    inc("dve")
                    ck.put(f"yev_{sg}_{hc}", val("dve"))
                    nxt = (sg, hc + 2) if hc < 14 else (sg + 1, (hc + 2) % 16)
                    if nxt[0] < NSUB:
                        ck.put(f"ypfree_{nxt[0]}_{nxt[1]}", val("dve"))

        @b.gpsimd
        def _(g):
            g.wait_ge(sem["sq"], ck.get("ln_done"))
            offr = g.value_load(T["sb"]["off_s"][0:1, 0:1])
            g.dma_start(out=T["xno_d"][:],
                        in_=xn_d[:, bass.ds(offr, TOK)]).then_inc(sem["gq"], 16)
            inc("gq", 16)
            ck.put("xno_cp", val("gq"))
            g.dma_start(out=T["xo_d"][:],
                        in_=T["xT"][:, bass.ds(offr, TOK)]).then_inc(sem["gq"], 16)
            inc("gq", 16)
            ck.put("xo_cp", val("gq"))
            for sg in range(NSUB):
                for hc in range(16):
                    par = hc % 2
                    g.wait_ge(sem["dve"], ck.get(f"yev_{sg}_{hc}"))
                    if sg >= 1:
                        g.wait_ge(sem["gq"], ck.get(f"yw_{sg - 1}_{hc}"))
                    g.dma_start(out=yd[hc * 128:(hc + 1) * 128, :],
                                in_=yst[:, par * 1024:par * 1024 + 1024],
                                accum_op=(ALU.add if sg > 0 else ALU.bypass)
                                ).then_inc(sem["gq"], 16)
                    inc("gq", 16)
                    ck.put(f"yw_{sg}_{hc}", val("gq"))
                    nxt = (sg, hc + 2) if hc < 14 else (sg + 1, (hc + 2) % 16)
                    if nxt[0] < NSUB:
                        ck.put(f"ywfree_{nxt[0]}_{nxt[1]}", val("gq"))
            ck.put("y_done", val("gq"))


def _emit_epilogue(ctl, T):
    nc, sem, ck, inc, val = ctl["nc"], ctl["sem"], ctl["ck"], ctl["inc"], ctl["val"]
    sb = T["sb"]
    ye_s, xe_s, se_s, ge_s, oe_s = (sb["ye_s"], sb["xe_s"], sb["se_s"],
                                    sb["ge_s"], sb["oe_s"])
    NSUB = 16

    with nc.Block() as b:
        def emit_ow(sc, hc):
            sc.wait_ge(sem["dve"], ck.get(f"eo_{hc}"))
            sc.dma_start(out=T["outT"][hc * 128:(hc + 1) * 128, :],
                         in_=oe_s[:, (hc % 2) * 1024:(hc % 2) * 1024 + 1024]
                         ).then_inc(sem["sq"], 16)
            inc("sq", 16)
            ck.put(f"ow_{hc}", val("sq"))

        @b.scalar
        def _(sc):
            sc.preload_act_table(AF.Silu)
            for hc in range(16):
                par = hc % 2
                sc.wait_ge(sem["gq"], ck.get(f"yw_{NSUB - 1}_{hc}"))
                if hc >= 2:
                    emit_ow(sc, hc - 2)
                sc.dma_start(out=ye_s[:, par * 1024:par * 1024 + 1024],
                             in_=T["yd"][hc * 128:(hc + 1) * 128, :]
                             ).then_inc(sem["sq"], 16)
                inc("sq", 16)
                sc.dma_start(out=se_s[:, par * 1024:par * 1024 + 1024],
                             in_=T["s_d"][hc * 128:(hc + 1) * 128, :]
                             ).then_inc(sem["sq"], 16)
                inc("sq", 16)
                ck.put(f"ep_ld{hc}", val("sq"))
                sc.wait_ge(sem["sq"], ck.get(f"ep_ld{hc}"))
                sc.activation(ge_s[:, par * 1024:par * 1024 + 1024],
                              ye_s[:, par * 1024:par * 1024 + 1024],
                              AF.Silu).then_inc(sem["act"], 1)
                inc("act")
                ck.put(f"sil_{hc}", val("act"))
            emit_ow(sc, 14)
            emit_ow(sc, 15)
            ck.put("out_done", val("sq"))

        @b.sync
        def _(sy):
            sy.wait_ge(sem["gq"], ck.get("xo_cp"))
            for hc in range(16):
                par = hc % 2
                if hc >= 2:
                    sy.wait_ge(sem["dve"], ck.get(f"eo_{hc - 2}"))
                sy.dma_start(
                    out=xe_s[:, par * 1024:par * 1024 + 1024],
                    in_=T["xo_d"][hc * 128:(hc + 1) * 128, :],
                ).then_inc(sem["sy"], 16)
                inc("sy", 16)
                ck.put(f"ep_x{hc}", val("sy"))

        @b.vector
        def _(v):
            for hc in range(16):
                par = hc % 2
                sl = slice(par * 1024, par * 1024 + 1024)
                v.wait_ge(sem["act"], ck.get(f"sil_{hc}"))
                v.wait_ge(sem["sy"], ck.get(f"ep_x{hc}"))
                if hc >= 2:
                    v.wait_ge(sem["sq"], ck.get(f"ow_{hc - 2}"))
                v.tensor_tensor(ge_s[:, sl], se_s[:, sl], ge_s[:, sl], ALU.mult)
                v.tensor_tensor(oe_s[:, sl], xe_s[:, sl], ge_s[:, sl],
                                ALU.add).then_inc(sem["dve"], 1)
                inc("dve")
                ck.put(f"eo_{hc}", val("dve"))

    if "ffn_pp" in ctl:
        ctl["ffn_pp"].close()


def _final(nc, sem, ck, sname, ckey):
    with nc.Block() as b:
        @b.gpsimd
        def _(g):
            g.wait_ge(sem[sname], ck.get(ckey))


def _stage_inputs(inputs):
    x = np.asarray(inputs["L0_attn_hidden_states"])
    Wq = np.asarray(inputs["L0_attn_Wq"])
    Wk = np.asarray(inputs["L0_attn_Wk"])
    Wv = np.asarray(inputs["L0_attn_Wv"])
    Wo = np.asarray(inputs["L0_attn_Wo"])
    bo = np.asarray(inputs["L0_attn_bo"])
    gwp = np.asarray(inputs["L0_ffn_gate_W_packed"])
    gsc = np.asarray(inputs["L0_ffn_gate_scales"])
    uwp = np.asarray(inputs["L0_ffn_up_W_packed"])
    usc = np.asarray(inputs["L0_ffn_up_scales"])
    dwp = np.asarray(inputs["L0_ffn_down_W_packed"])
    dsc = np.asarray(inputs["L0_ffn_down_scales"])

    bf = ml_dtypes.bfloat16
    qg = np.arange(S, dtype=np.float64)
    ka_t = np.stack([
        np.ones(S), np.ones(S),
        128.0 * (np.arange(S) // 128), 128.0 * (np.arange(S) // 128),
        (np.arange(S) % 128).astype(np.float64),
        (np.arange(S) % 128).astype(np.float64),
    ]).astype(bf)
    d_in_head = np.arange(128) % 64
    inv_freq = 1.0 / 10000.0 ** (np.arange(0, DH, 2, dtype=np.float64) / DH)
    ang = np.arange(S, dtype=np.float64)[None, :] * inv_freq[d_in_head // 2][:, None]
    rcos_t = np.cos(ang).astype(np.float32)
    rsin_t = np.sin(ang).astype(np.float32)
    kl = np.arange(128)
    mtri_t = np.where(kl[None, :] >= kl[:, None], 0.0, -1e6).astype(np.float32)
    P64 = np.zeros((64, 64), np.float32)
    j = np.arange(32)
    P64[2 * j, 2 * j + 1] = -1.0
    P64[2 * j + 1, 2 * j] = 1.0
    PT = P64.T
    psw_t = np.zeros((128, 128), np.float32)
    psw_t[0:64, 0:64] = PT
    psw_t[64:128, 64:128] = PT
    psw_t = psw_t.astype(bf)
    idn_t = np.eye(128, dtype=bf)

    Wq_p = np.ascontiguousarray(Wq[PERM_D])
    Wk_p = np.ascontiguousarray(Wk[PERM_D])
    Wv_p = np.ascontiguousarray(Wv[PERM_D])
    Wo_p = np.ascontiguousarray(Wo[:, PERM_D].astype(np.float32))
    bo_p = np.ascontiguousarray(bo[PERM_D].reshape(16, 128).T)
    gwp_p = gwp[PERM_N]
    gsc_p = gsc[PERM_N]
    uwp_p = uwp[PERM_N]
    usc_p = usc[PERM_N]
    dwp_p = dwp[PERM_D]
    dsc_p = dsc[PERM_D]

    in_maps = []
    for c in range(8):
        b_, half = c >> 1, c & 1
        heads = np.arange(16 * half, 16 * half + 16)
        qa_t = np.empty((NP, 6, S), dtype=bf)
        for i, h in enumerate(heads):
            slope = 2.0 ** (-(float(h) + 1.0) * 8.0 / 32.0)
            vv = -slope * qg
            ahi = vv.astype(bf)
            alo = (vv - ahi.astype(np.float64)).astype(bf)
            shi = np.array(slope).astype(bf)
            slo = np.array(slope - shi.astype(np.float64)).astype(bf)
            qa_t[i, 0] = ahi
            qa_t[i, 1] = alo
            qa_t[i, 2] = shi
            qa_t[i, 3] = slo
            qa_t[i, 4] = shi
            qa_t[i, 5] = slo
        m = {
            "xT": np.ascontiguousarray(x[b_].T[PERM_D]),
            "wq": np.ascontiguousarray(Wq_p[:, heads[0] * DH:heads[-1] * DH + DH]),
            "wk": np.ascontiguousarray(Wk_p[:, half * 256:half * 256 + 256]),
            "wv": np.ascontiguousarray(Wv_p[:, half * 256:half * 256 + 256]),
            "wo": Wo_p,
            "bo": bo_p,
            "qa": qa_t,
            "ka": ka_t,
            "rcos": rcos_t,
            "rsin": rsin_t,
            "mtri": mtri_t,
            "psw": psw_t,
            "idn": idn_t,
            "offi": np.array([[1024 * half]], np.int32),
            "gp": np.ascontiguousarray(gwp_p[c * 2048:(c + 1) * 2048]),
            "gs": np.ascontiguousarray(gsc_p[c * 2048:(c + 1) * 2048]),
            "upk": np.ascontiguousarray(uwp_p[c * 2048:(c + 1) * 2048]),
            "us": np.ascontiguousarray(usc_p[c * 2048:(c + 1) * 2048]),
            "dp": np.ascontiguousarray(dwp_p[:, c * 1024:(c + 1) * 1024]),
            "ds": np.ascontiguousarray(dsc_p[:, c * 64:(c + 1) * 64]),
        }
        in_maps.append(m)
    return in_maps


def build_twopass(stage=STAGE_FULL):
    _, ck1 = build_nc(stage, None)
    nc, _ = build_nc(stage, ck1)
    return nc


def kernel(**inputs):
    in_maps = _stage_inputs(inputs)
    nc = build_twopass(STAGE_FULL)
    res = run_bass_kernel_spmd(nc, in_maps, core_ids=list(range(8)))
    out = np.empty((4, S, D), np.float32)
    inv = np.empty(D, np.int64)
    inv[PERM_D] = np.arange(D)
    for c in range(8):
        b_, half = c >> 1, c & 1
        arr = res.results[c]["outT"]
        out[b_, half * TOK:(half + 1) * TOK, :] = arr.T[:, inv]
    return out
